# version 17
# baseline (speedup 1.0000x reference)
import numpy as np
import concourse.bass as bass
import concourse.mybir as mybir
from concourse.bass_utils import run_bass_kernel_spmd

F32 = mybir.dt.float32
BF16 = mybir.dt.bfloat16
AF = mybir.ActivationFunctionType
ALU = mybir.AluOpType
f32 = np.float32

NCORES = 8
D = 2048
HID = 8192
CH = D // 128
HC = HID // 128
TOK = 2048
T = 512
NT = TOK // T
SCALE = 128 ** -0.5
INV_D = 1.0 / D
EPS = 1e-5
FWHT_SCALE = HID ** -0.5

ENGS = ("sync", "gpsimd", "vector", "scalar", "tensor")
DMA_SEMS = {"LD", "LG", "ST"}


class Plan:
    def __init__(self):
        self.ops = {e: [] for e in ENGS}
        self.cnt = {}
        self.last_wait = {e: {} for e in ENGS}

    def op(self, eng, fn, waits=(), inc=None):
        mark = None
        amt = 0
        if inc is not None:
            amt = 16 if inc in DMA_SEMS else 1
            self.cnt[inc] = self.cnt.get(inc, 0) + amt
            mark = (inc, self.cnt[inc])
        real_waits = []
        lw = self.last_wait[eng]
        for w in waits:
            if w is None:
                continue
            s, v = w
            if lw.get(s, -1) >= v:
                continue
            lw[s] = v
            real_waits.append((s, v))
        self.ops[eng].append((real_waits, fn, inc, amt))
        return mark

    def emit(self, eng_obj, eng_name, sems):
        for waits, fn, inc, amt in self.ops[eng_name]:
            for (s, v) in waits:
                eng_obj.wait_ge(sems[s], v)
            ins = fn(eng_obj)
            if inc is not None:
                ins.then_inc(sems[inc], amt)


def _hadamard128():
    pc = np.array([bin(v).count("1") for v in range(128)])
    i = np.arange(128)
    return ((-1.0) ** pc[(i[:, None] & i[None, :])]).astype(np.float32)


def build_program(debug=False):
    nc = bass.Bass("TRN2", target_bir_lowering=False)

    xT = nc.dram_tensor("xT", [D, TOK], F32, kind="ExternalInput")
    w1blk = nc.dram_tensor("w1blk", [HC, 128, CH, 128], F32, kind="ExternalInput")
    w3blk = nc.dram_tensor("w3blk", [HC, 128, CH, 128], F32, kind="ExternalInput")
    wqblk = nc.dram_tensor("wqblk", [CH, 128, CH, 128], F32, kind="ExternalInput")
    wkblk = nc.dram_tensor("wkblk", [CH, 128, CH, 128], F32, kind="ExternalInput")
    wvblk = nc.dram_tensor("wvblk", [CH, 128, CH, 128], F32, kind="ExternalInput")
    woblk = nc.dram_tensor("woblk", [CH, 128, CH, 128], F32, kind="ExternalInput")
    w2ts = nc.dram_tensor("w2ts", [HID, 256], F32, kind="ExternalInput")
    bqh = nc.dram_tensor("bqh", [128, CH], F32, kind="ExternalInput")
    bkh = nc.dram_tensor("bkh", [128, CH], F32, kind="ExternalInput")
    bvh = nc.dram_tensor("bvh", [128, CH], F32, kind="ExternalInput")
    boh = nc.dram_tensor("boh", [128, CH], F32, kind="ExternalInput")
    h128h = nc.dram_tensor("h128h", [128, 128], F32, kind="ExternalInput")
    selh = nc.dram_tensor("selh", [8, 1], F32, kind="ExternalInput")
    outT = nc.dram_tensor("outT", [D, TOK], F32, kind="ExternalOutput")

    dbg_kind = "ExternalOutput" if debug else "Internal"
    xA = nc.dram_tensor("xA", [D, TOK], F32, kind=dbg_kind)
    cumD = nc.dram_tensor("cumD", [D, TOK], F32, kind=dbg_kind)
    w1c = nc.dram_tensor("w1c", [HC, 128, CH, 128], BF16, kind="Internal")
    w3c = nc.dram_tensor("w3c", [HC, 128, CH, 128], BF16, kind="Internal")
    wqc = nc.dram_tensor("wqc", [CH, 128, CH, 128], BF16, kind="Internal")
    wkc = nc.dram_tensor("wkc", [CH, 128, CH, 128], BF16, kind="Internal")
    wvc = nc.dram_tensor("wvc", [CH, 128, CH, 128], BF16, kind="Internal")
    woc = nc.dram_tensor("woc", [CH, 128, CH, 128], BF16, kind="Internal")
    w2loc = nc.dram_tensor("w2loc", [256, HID], BF16, kind="Internal")
    w2p = nc.dram_tensor("w2p", [2048, HID], BF16, kind="Internal",
                         addr_space="Shared")
    w2pd = (nc.dram_tensor("w2pd", [2048, HID], BF16, kind="ExternalOutput")
            if debug else None)
    cc_in = nc.dram_tensor("cc_in", [1, D], F32, kind="Internal")
    cc_out = nc.dram_tensor("cc_out", [8, D], F32, kind="Internal",
                            addr_space="Shared")
    row_scr = nc.dram_tensor("row_scr", [1, D], F32, kind="Internal")

    xT_v = xT.rearrange("(c p) t -> p c t", p=128)
    xA_v = xA.rearrange("(c p) t -> p c t", p=128)
    cum_v = cumD.rearrange("(c p) t -> p c t", p=128)
    out_v = outT.rearrange("(c p) t -> p c t", p=128)
    w2ts_v = w2ts.rearrange("(jc p) e -> p jc e", p=128)

    _probe = nc.alloc_sbuf_tensor("base_probe", [128, 32], mybir.dt.uint8)
    _base = nc.lookup_mloc(_probe).addr + 32
    alloc_top = [_base]

    def sa(name, shape, dtype, at=None):
        elt = 2 if dtype == BF16 else 4
        nbytes = int(np.prod(shape[1:])) * elt
        if at is None:
            off = (alloc_top[0] + 31) & ~31
            alloc_top[0] = off + nbytes
        else:
            off = at
        return nc.alloc_sbuf_tensor_at(name, shape, dtype, offset=off)

    xsb = sa("xsb", [128, CH * T], F32)
    xnb = sa("xnb", [128, CH * T], BF16)
    gsb = sa("gsb", [128, HC * T], BF16)
    accs = sa("accs", [128, CH * T], F32)
    wb = [sa(f"wb{i}", [128, CH * 128], BF16) for i in range(4)]
    w2sb = [sa(f"w2sb{i}", [128, 4096], BF16) for i in range(2)]
    base_after = alloc_top[0]

    def off_of(t_):
        return t_.manual_sbuf_range[0]

    pstrip = [
        sa("pstripA", [128, HC * 128], F32, at=off_of(gsb)),
        sa("pstripB", [128, HC * 128], F32, at=off_of(accs)),
    ]
    bstg = sa("bstg", [128, HC * 128], BF16, at=off_of(xsb))
    stg2 = sa("stg2", [128, HC * 128], BF16, at=off_of(xsb) + 16384)
    ctx = sa("ctx", [128, CH * T], BF16, at=off_of(gsb) + 32768)
    gset = sa("gset", [8, D], F32, at=off_of(accs))
    gwt = sa("gwt", [8, D], F32, at=off_of(accs) + 8192)
    flat8 = sa("flat8", [1, 8 * D], F32, at=off_of(gsb))
    rowsum = sa("rowsum", [1, D], F32, at=off_of(accs) + 16384)

    alloc_top[0] = base_after
    scan1 = sa("scan1", [128, T], F32)
    onecol = sa("onecol", [128, 1], BF16)
    onerow = sa("onerow", [1, 128], BF16)
    h128b = sa("h128b", [128, 128], BF16)
    halfc = sa("halfc", [128, 1], F32)
    sccol = sa("sccol", [128, 1], F32)
    bqs = sa("bqs", [128, CH], F32)
    bks = sa("bks", [128, CH], F32)
    bvs = sa("bvs", [128, CH], F32)
    bos = sa("bos", [128, CH], F32)
    sel_sb = sa("sel_sb", [8, 1], F32)
    offs = sa("offs", [128, CH], F32)
    car = [sa(f"car{i}", [128, CH], F32) for i in range(2)]
    csum = sa("csum", [128, 1], F32)
    sqb = [sa(f"sqb{i}", [128, T], BF16) for i in range(2)]
    s1b = [sa(f"s1b{i}", [128, T], F32) for i in range(2)]
    xstage = [sa(f"xstage{i}", [128, T], BF16, at=off_of(s1b[i]))
              for i in range(2)]
    rb_ = sa("rbS", [128, T], F32)
    mb_ = sa("mbS", [128, T], F32)
    eb_ = sa("ebS", [128, T], F32)
    rb = [rb_, rb_]
    mb = [mb_, mb_]
    eb = [eb_, eb_]
    cumb = [sa(f"cumb{i}", [128, T], F32) for i in range(2)]
    rows = sa("rows", [1, 4 * T], F32)
    rows_b = sa("rows_b", [1, 2 * T], BF16)
    assert alloc_top[0] <= _base + 212800 - 32, \
        f"sbuf over budget: {alloc_top[0] - _base}"

    PS = [nc.alloc_psum_tensor(f"P{i}", [128, T], F32) for i in range(8)]

    P = Plan()
    ps_cons = {i: None for i in range(8)}
    wb_cons = {i: None for i in range(4)}
    wb_stm = {i: None for i in range(4)}
    w2_cons = {i: None for i in range(2)}
    wcache = {k: {} for k in ("w1", "w3", "wq", "wk", "wv", "wo")}

    P.op("vector", lambda E: E.memset(scan1[:], 1.0))
    P.op("vector", lambda E: E.memset(onecol[:], 1.0))
    P.op("vector", lambda E: E.memset(onerow[:], 1.0))
    P.op("vector", lambda E: E.memset(halfc[:], 0.5))
    P.op("vector", lambda E: E.memset(sccol[:], SCALE))
    P.op("vector", lambda E: E.memset(car[0][:], 0.0), inc="VE")
    m_small = []
    for dst, src in ((bqs, bqh), (bks, bkh), (bvs, bvh), (bos, boh),
                     (sel_sb, selh)):
        m_small.append(P.op("sync",
                            lambda E, d=dst, s=src: E.dma_start(d[:], s.ap()),
                            inc="LD"))
    m_h128 = P.op("gpsimd", lambda E: E.dma_start(h128b[:], h128h.ap()),
                  inc="LG")

    prep_store_marks = []
    prev_evac = None
    prev_pe = None
    for ls in range(2):
        src = w2ts_v[:, :, ls * 128:(ls + 1) * 128]
        lw = [prev_pe] if ls else []
        mload = P.op("sync", lambda E, s=src: E.dma_start(pstrip[0][:], s),
                     waits=lw, inc="LD")
        cur = 0
        mst = mload
        last_bf = None
        for st_i, dch in enumerate([1, 2, 4, 8, 16, 32]):
            sv = pstrip[cur].rearrange("p (q two t) -> p q two t", two=2,
                                       t=dch * 128)
            dv = pstrip[1 - cur].rearrange("p (q two t) -> p q two t", two=2,
                                           t=dch * 128)
            w = [mst] if st_i == 0 else []
            P.op("vector",
                 lambda E, d=dv, s=sv: E.tensor_tensor(
                     d[:, :, 0, :], s[:, :, 0, :], s[:, :, 1, :], ALU.add),
                 waits=w, inc="VE")
            last_bf = P.op("vector",
                           lambda E, d=dv, s=sv: E.tensor_tensor(
                               d[:, :, 1, :], s[:, :, 0, :], s[:, :, 1, :],
                               ALU.subtract),
                           inc="VE")
            cur = 1 - cur
        mconv = P.op("vector",
                     lambda E, s=pstrip[cur]: E.tensor_copy(bstg[:], s[:]),
                     waits=[last_bf, prev_pe], inc="VE")
        for q in range(16):
            w = [mconv, m_h128, prev_evac]
            mpe = P.op("tensor",
                       lambda E, q=q: E.matmul(PS[6][:], h128b[:],
                                               bstg[:, q * 512:(q + 1) * 512],
                                               start=True, stop=True),
                       waits=w, inc="PE")
            prev_pe = mpe
            ew = [mpe] + ([prep_store_marks[-1]] if (q == 0 and prep_store_marks)
                          else [])
            prev_evac = P.op("scalar",
                             lambda E, q=q: E.activation(
                                 stg2[:, q * 512:(q + 1) * 512], PS[6][:],
                                 AF.Identity, scale=f32(FWHT_SCALE)),
                             waits=ew, inc="SE")
        mw = P.op("sync",
                  lambda E, ls=ls: E.dma_start(
                      w2loc[ls * 128:(ls + 1) * 128, :], stg2[:]),
                  waits=[prev_evac], inc="ST")
        prep_store_marks.append(mw)
        ps_cons[6] = prev_evac
    m_ag1 = P.op("gpsimd",
                 lambda E: E.collective_compute(
                     "AllGather", ALU.bypass, replica_groups=[list(range(8))],
                     ins=[w2loc.ap().opt()], outs=[w2p.ap().opt()]),
                 waits=prep_store_marks, inc="CC1")

    if w2pd is not None:
        P.op("sync", lambda E: E.dma_start(w2pd.ap(), w2p.ap()),
             waits=[("CC1", 1)], inc="ST")

    sq_pe = []

    def ln_stats(src_sb, c, extra_waits=()):
        sl = slice(c * T, (c + 1) * T)
        mcopy = P.op("scalar",
                     lambda E, sl=sl, b=xstage[c % 2]: E.copy(b[:], src_sb[:, sl]),
                     waits=list(extra_waits) + ([sq_pe[-4]] if len(sq_pe) >= 4
                                                else []),
                     inc="SE")
        msq = P.op("scalar",
                   lambda E, sl=sl, b=sqb[c % 2]: E.activation(
                       b[:], src_sb[:, sl], AF.Square),
                   waits=[sq_pe[-3]] if len(sq_pe) >= 3 else [], inc="SE")
        m1 = P.op("tensor",
                  lambda E, c=c, b=xstage[c % 2]: E.matmul(
                      PS[6][0:1, :], onecol[:], b[:],
                      start=(c == 0), stop=(c == CH - 1)),
                  waits=[mcopy] + ([ps_cons[6]] if c == 0 else []), inc="PE")
        m2 = P.op("tensor",
                  lambda E, c=c, b=sqb[c % 2]: E.matmul(
                      PS[7][0:1, :], onecol[:], b[:],
                      start=(c == 0), stop=(c == CH - 1)),
                  waits=[msq] + ([ps_cons[7]] if c == 0 else []), inc="PE")
        sq_pe.append(m1)
        sq_pe.append(m2)
        return m1, m2

    def ln_finish(src_sb, sum_marks):
        sum1, sum2 = sum_marks
        r_m = rows[0:1, 0:T]
        r_v = rows[0:1, T:2 * T]
        r_sd = rows[0:1, 2 * T:3 * T]
        r_a = rows[0:1, 3 * T:4 * T]
        r_bn = rows[0:1, T:2 * T]
        r_t = rows[0:1, 2 * T:3 * T]
        mm1 = P.op("scalar",
                   lambda E: E.activation(r_m, PS[6][0:1, :], AF.Identity,
                                          scale=f32(INV_D)),
                   waits=[sum1], inc="SE")
        mm2 = P.op("scalar",
                   lambda E: E.activation(r_v, PS[7][0:1, :], AF.Identity,
                                          scale=f32(INV_D)),
                   waits=[sum2], inc="SE")
        ps_cons[6] = mm1
        ps_cons[7] = mm2
        mt = P.op("vector", lambda E: E.tensor_tensor(r_t, r_m, r_m, ALU.mult),
                  waits=[mm1], inc="VE")
        mv = P.op("vector",
                  lambda E: E.tensor_tensor(r_v, r_v, r_t, ALU.subtract),
                  waits=[mm2, mt], inc="VE")
        msd = P.op("scalar",
                   lambda E: E.activation(r_sd, r_v, AF.Sqrt, bias=f32(EPS)),
                   waits=[mv], inc="SE")
        mra = P.op("vector", lambda E: E.reciprocal(r_a, r_sd), waits=[msd],
                   inc="VE")
        mbn0 = P.op("vector",
                    lambda E: E.tensor_tensor(r_t, r_m, r_a, ALU.mult),
                    waits=[mra], inc="VE")
        mbn = P.op("vector",
                   lambda E: E.tensor_scalar(r_bn, r_t, -1.0, None, ALU.mult),
                   waits=[mbn0], inc="VE")
        mab = P.op("vector", lambda E: E.tensor_copy(rows_b[0:1, 0:T], r_a),
                   waits=[mra], inc="VE")
        mbb = P.op("vector",
                   lambda E: E.tensor_copy(rows_b[0:1, T:2 * T], r_bn),
                   waits=[mbn], inc="VE")
        mba = P.op("tensor",
                   lambda E: E.matmul(PS[0][:], onerow[:], rows_b[0:1, 0:T],
                                      start=True, stop=True),
                   waits=[mab, ps_cons[0]], inc="PE")
        mbbc = P.op("tensor",
                    lambda E: E.matmul(PS[1][:], onerow[:],
                                       rows_b[0:1, T:2 * T],
                                       start=True, stop=True),
                    waits=[mbb, ps_cons[1]], inc="PE")
        am = None
        for c in range(CH):
            sl = slice(c * T, (c + 1) * T)
            P.op("vector",
                 lambda E, sl=sl, b=rb[c % 2]: E.tensor_tensor(
                     b[:], src_sb[:, sl], PS[0][:], ALU.mult),
                 waits=[mba], inc="VE")
            am = P.op("vector",
                      lambda E, sl=sl, b=rb[c % 2]: E.tensor_tensor(
                          xnb[:, sl], b[:], PS[1][:], ALU.add),
                      waits=[mbbc], inc="VE")
        ps_cons[0] = am
        ps_cons[1] = am
        return am

    def ln(src_sb, pre_waits):
        sm = None
        for c in range(CH):
            sm = ln_stats(src_sb, c, extra_waits=pre_waits if c == 0 else ())
        return ln_finish(src_sb, sm)

    g_marks = {}

    def wload(eng, slot, kind, f32blk, cblk, mc, first):
        if first:
            ml = P.op(eng,
                      lambda E, s=slot, mc=mc: E.dma_start(wb[s][:],
                                                           f32blk[mc]),
                      waits=[wb_cons[slot], wb_stm[slot]], inc="LG")
            mst = P.op("scalar",
                       lambda E, s=slot, mc=mc: E.dma_start(cblk[mc],
                                                            wb[s][:]),
                       waits=[ml], inc="ST")
            wcache[kind][mc] = mst
            wb_stm[slot] = mst
        else:
            ml = P.op(eng,
                      lambda E, s=slot, mc=mc: E.dma_start(wb[s][:], cblk[mc]),
                      waits=[wb_cons[slot], wb_stm[slot], wcache[kind][mc]],
                      inc="LG")
        return ml

    def ffn(src_res, dst_sb, store_view, apply_mark, dst_free_wait,
            post_ec=None, first=False):
        for mc in range(HC):
            s0, s1_ = (2 * mc) % 4, (2 * mc + 1) % 4
            l1 = wload("gpsimd", s0, "w1", w1blk, w1c, mc, first)
            l3 = wload("gpsimd", s1_, "w3", w3blk, w3c, mc, first)
            pA = 0 if mc % 2 == 0 else 2
            pB = 1 if mc % 2 == 0 else 3
            mmA = mmB = None
            for kc in range(CH):
                ksl = slice(kc * T, (kc + 1) * T)
                w = [l1, apply_mark, ps_cons[pA]] if kc == 0 else []
                mmA = P.op("tensor",
                           lambda E, kc=kc, s=s0, ksl=ksl, pA=pA: E.matmul(
                               PS[pA][:], wb[s][:, kc * 128:(kc + 1) * 128],
                               xnb[:, ksl], start=(kc == 0),
                               stop=(kc == CH - 1)),
                           waits=w, inc="PE")
            for kc in range(CH):
                ksl = slice(kc * T, (kc + 1) * T)
                w = [l3, ps_cons[pB]] if kc == 0 else []
                mmB = P.op("tensor",
                           lambda E, kc=kc, s=s1_, ksl=ksl, pB=pB: E.matmul(
                               PS[pB][:], wb[s][:, kc * 128:(kc + 1) * 128],
                               xnb[:, ksl], start=(kc == 0),
                               stop=(kc == CH - 1)),
                           waits=w, inc="PE")
            wb_cons[s0], wb_cons[s1_] = mmA, mmB
            wsl = [mmA] + ([g_marks[mc - 2]] if mc - 2 in g_marks else [])
            msilu = P.op("scalar",
                         lambda E, mc=mc, pA=pA: E.activation(
                             s1b[mc % 2][:], PS[pA][:], AF.Silu),
                         waits=wsl, inc="SE")
            ps_cons[pA] = msilu
            mg = P.op("vector",
                      lambda E, mc=mc, pB=pB: E.tensor_tensor(
                          gsb[:, mc * T:(mc + 1) * T], s1b[mc % 2][:],
                          PS[pB][:], ALU.mult),
                      waits=[msilu, mmB], inc="VE")
            ps_cons[pB] = mg
            g_marks[mc] = mg
        last_g = g_marks[HC - 1]
        ev_marks = []
        for ec in range(CH):
            pD = 4 if ec % 2 == 0 else 5
            hl = []
            for h in range(2):
                lw = [w2_cons[h]]
                if ec == 0 and h == 0:
                    lw.append(("CC1", 1))
                ml = P.op("sync",
                          lambda E, ec=ec, h=h: E.dma_start(
                              w2sb[h][:],
                              w2p[ec * 128:(ec + 1) * 128,
                                  h * 4096:(h + 1) * 4096]),
                          waits=lw, inc="LD")
                hl.append(ml)
            mmD = None
            for jc in range(HC):
                h = jc // 32
                jl = jc % 32
                w = []
                if jc == 0:
                    w = [hl[0], last_g, ps_cons[pD]]
                elif jc == 32:
                    w = [hl[1]]
                mmD = P.op("tensor",
                           lambda E, jc=jc, h=h, jl=jl, pD=pD: E.matmul(
                               PS[pD][:], w2sb[h][:, jl * 128:(jl + 1) * 128],
                               gsb[:, jc * T:(jc + 1) * T],
                               start=(jc == 0), stop=(jc == HC - 1)),
                           waits=w, inc="PE")
                if jc == 31:
                    w2_cons[0] = mmD
            w2_cons[1] = mmD
            sl = slice(ec * T, (ec + 1) * T)
            wev = [mmD] + ([dst_free_wait] if ec == 0 else [])
            mev = P.op("vector",
                       lambda E, sl=sl, pD=pD: E.scalar_tensor_tensor(
                           dst_sb[:, sl], PS[pD][:], halfc[:],
                           src_res[:, sl], ALU.mult, ALU.add),
                       waits=wev, inc="VE")
            ps_cons[pD] = mev
            ev_marks.append(mev)
            if post_ec is not None:
                post_ec(ec, mev)
        mstore = P.op("sync", lambda E: E.dma_start(store_view, dst_sb[:]),
                      waits=[ev_marks[-1]], inc="ST")
        return mstore, ev_marks[-1]

    def proj_mm(blk, cblk, kind, mc, slot, pbank, rhs_sb, first_waits,
                first=False):
        ml = wload("gpsimd", slot, kind, blk, cblk, mc, first)
        mm = None
        for kc in range(CH):
            ksl = slice(kc * T, (kc + 1) * T)
            w = [ml, ps_cons[pbank]] + list(first_waits) if kc == 0 else []
            mm = P.op("tensor",
                      lambda E, kc=kc, s=slot, ksl=ksl, pb=pbank: E.matmul(
                          PS[pb][:], wb[s][:, kc * 128:(kc + 1) * 128],
                          rhs_sb[:, ksl], start=(kc == 0),
                          stop=(kc == CH - 1)),
                      waits=w, inc="PE")
        wb_cons[slot] = mm
        return mm

    xa_store = []
    cum_store = []
    prev_ffn_ev = None
    prev_xa_store = None
    am_next = None
    for t in range(NT):
        tsl = slice(t * T, (t + 1) * T)
        if t == 0:
            mload = P.op("sync",
                         lambda E: E.dma_start(xsb[:], xT_v[:, :, 0:T]),
                         waits=list(prep_store_marks), inc="LD")
            am = ln(xsb, [mload])
        else:
            am = am_next
        ln2_marks = [None]

        def post_ec(ec, mev, _m=ln2_marks):
            _m[0] = ln_stats(accs, ec, extra_waits=[mev])

        mstore, last_ev = ffn(xsb, accs, xA_v[:, :, tsl], am, prev_xa_store,
                              post_ec=post_ec, first=(t == 0))
        xa_store.append(mstore)
        prev_xa_store = mstore
        am2 = ln_finish(accs, ln2_marks[0])
        mload_next = None
        if t + 1 < NT:
            mload_next = P.op("sync",
                              lambda E, t=t: E.dma_start(
                                  xsb[:], xT_v[:, :, (t + 1) * T:(t + 2) * T]),
                              waits=[last_ev], inc="LD")
        ln1_marks = None
        for mc in range(CH):
            pK = 0 if mc % 2 == 0 else 2
            pV = 1 if mc % 2 == 0 else 3
            mk = proj_mm(wkblk, wkc, "wk", mc, (2 * mc) % 4, pK, xnb, [am2],
                         first=(t == 0))
            mv_ = proj_mm(wvblk, wvc, "wv", mc, (2 * mc + 1) % 4, pV, xnb, [],
                          first=(t == 0))
            mr = P.op("vector",
                      lambda E, mc=mc, pK=pK: E.tensor_scalar(
                          rb[mc % 2][:], PS[pK][:], bks[:, mc:mc + 1], 0.0,
                          ALU.add, ALU.max),
                      waits=[mk], inc="VE")
            mn = P.op("vector",
                      lambda E, mc=mc, pK=pK: E.tensor_scalar(
                          mb[mc % 2][:], PS[pK][:], bks[:, mc:mc + 1], 0.0,
                          ALU.add, ALU.min),
                      inc="VE")
            ps_cons[pK] = mn
            me = P.op("scalar",
                      lambda E, mc=mc: E.activation(eb[mc % 2][:],
                                                    mb[mc % 2][:], AF.Exp),
                      waits=[mn], inc="SE")
            mphi = P.op("vector",
                        lambda E, mc=mc: E.tensor_tensor(
                            mb[mc % 2][:], eb[mc % 2][:], rb[mc % 2][:],
                            ALU.add),
                        waits=[me], inc="VE")
            mvv = P.op("vector",
                       lambda E, mc=mc, pV=pV: E.tensor_scalar(
                           rb[mc % 2][:], PS[pV][:], bvs[:, mc:mc + 1], None,
                           ALU.add),
                       waits=[mv_, mphi], inc="VE")
            ps_cons[pV] = mvv
            P.op("vector",
                 lambda E, mc=mc: E.tensor_tensor(
                     eb[mc % 2][:], mb[mc % 2][:], rb[mc % 2][:], ALU.mult),
                 inc="VE")
            P.op("vector",
                 lambda E, mc=mc: E.reduce_sum(csum[:], eb[mc % 2][:],
                                               mybir.AxisListType.X),
                 inc="VE")
            wsc = [cum_store[-2]] if len(cum_store) >= 2 else []
            cin_, cout_ = car[t % 2], car[(t + 1) % 2]
            mcum = P.op("vector",
                        lambda E, mc=mc, cin_=cin_: E.tensor_tensor_scan(
                            cumb[mc % 2][:], scan1[:], eb[mc % 2][:],
                            cin_[:, mc:mc + 1], ALU.mult, ALU.add),
                        waits=wsc, inc="VE")
            P.op("vector",
                 lambda E, mc=mc, cin_=cin_, cout_=cout_: E.tensor_tensor(
                     cout_[:, mc:mc + 1], csum[:], cin_[:, mc:mc + 1],
                     ALU.add),
                 inc="VE")
            mcs = P.op("sync",
                       lambda E, mc=mc, t=t: E.dma_start(
                           cum_v[:, mc:mc + 1, t * T:(t + 1) * T],
                           cumb[mc % 2][:]),
                       waits=[mcum], inc="ST")
            cum_store.append(mcs)
            if t + 1 < NT:
                ln1_marks = ln_stats(
                    xsb, mc,
                    extra_waits=[mload_next] if mc == 0 else ())
        if t + 1 < NT:
            am_next = ln_finish(xsb, ln1_marks)
        prev_ffn_ev = last_ev

    mloadC0 = P.op("sync",
                   lambda E: E.dma_start(xsb[:], xA_v[:, :, 0:T]),
                   waits=[prev_ffn_ev, xa_store[0]], inc="LD")
    cumC0 = []
    for mc in range(2):
        cumC0.append(P.op("sync",
                          lambda E, mc=mc: E.dma_start(
                              cumb[mc % 2][:], cum_v[:, mc:mc + 1, 0:T]),
                          waits=[cum_store[-1]] if mc == 0 else [], inc="LD"))
    ve_all = ("VE", P.cnt["VE"])
    cc_small = []
    for c in range(CH):
        mcc = P.op("scalar",
                   lambda E, c=c: E.dma_start(
                       cc_in[0:1, c * 128:(c + 1) * 128],
                       car[NT % 2][:, c:c + 1]),
                   waits=[ve_all] if c == 0 else [], inc="ST")
        cc_small.append(mcc)
    P.op("gpsimd",
         lambda E: E.collective_compute(
             "AllGather", ALU.bypass, replica_groups=[list(range(8))],
             ins=[cc_in.ap().opt()], outs=[cc_out.ap().opt()]),
         waits=cc_small, inc="CC2")
    pe_all = ("PE", P.cnt["PE"])
    m_gl = P.op("gpsimd", lambda E: E.dma_start(gset[:], cc_out.ap()),
                waits=[("CC2", 1), xa_store[-1], ve_all], inc="LG")
    m_gw = P.op("vector",
                lambda E: E.tensor_scalar(gwt[:], gset[:], sel_sb[:], None,
                                          ALU.mult),
                waits=[m_gl, m_small[4]], inc="VE")
    m_fl = P.op("scalar", lambda E: E.dma_start(flat8[0:1, :], gwt[:]),
                waits=[m_gw, pe_all], inc="ST")
    P.op("vector",
         lambda E: E.tensor_tensor(flat8[0:1, 0:4 * D], flat8[0:1, 0:4 * D],
                                   flat8[0:1, 4 * D:8 * D], ALU.add),
         waits=[m_fl], inc="VE")
    P.op("vector",
         lambda E: E.tensor_tensor(flat8[0:1, 0:2 * D], flat8[0:1, 0:2 * D],
                                   flat8[0:1, 2 * D:4 * D], ALU.add),
         inc="VE")
    m_a3 = P.op("vector",
                lambda E: E.tensor_tensor(rowsum[0:1, :], flat8[0:1, 0:D],
                                          flat8[0:1, D:2 * D], ALU.add),
                inc="VE")
    m_rs = P.op("scalar", lambda E: E.dma_start(row_scr.ap(), rowsum[0:1, :]),
                waits=[m_a3], inc="ST")
    m_off = None
    for c in range(CH):
        m_off = P.op("scalar",
                     lambda E, c=c: E.dma_start(
                         offs[:, c:c + 1],
                         row_scr[0:1, c * 128:(c + 1) * 128]),
                     waits=[m_rs] if c == 0 else [], inc="LD")

    prev_store = None
    prev_ev = prev_ffn_ev
    for t in range(NT):
        tsl = slice(t * T, (t + 1) * T)
        if t == 0:
            am = ln(xsb, [mloadC0])
        else:
            lw = [prev_ev, prev_store, xa_store[t]]
            mload = P.op("sync",
                         lambda E, t=t: E.dma_start(
                             xsb[:], xA_v[:, :, t * T:(t + 1) * T]),
                         waits=lw, inc="LD")
            am = ln(xsb, [mload])
        ctx_last = None
        ms_prev = {}
        for mc in range(CH):
            pQ = 0 if mc % 2 == 0 else 2
            mq = proj_mm(wqblk, wqc, "wq", mc, mc % 4, pQ, xnb, [am],
                         first=(t == 0))
            if t == 0 and mc < 2:
                mcl = cumC0[mc]
            else:
                wcl = [ms_prev[mc - 2]] if mc - 2 in ms_prev else []
                if t > 0 and mc < 2:
                    wcl = [ms_last_tile[mc - 2 + CH]]
                mcl = P.op("sync",
                           lambda E, mc=mc, t=t: E.dma_start(
                               cumb[mc % 2][:],
                               cum_v[:, mc:mc + 1, t * T:(t + 1) * T]),
                           waits=wcl, inc="LD")
            mr = P.op("vector",
                      lambda E, mc=mc, pQ=pQ: E.tensor_scalar(
                          rb[mc % 2][:], PS[pQ][:], bqs[:, mc:mc + 1], 0.0,
                          ALU.add, ALU.max),
                      waits=[mq], inc="VE")
            mn = P.op("vector",
                      lambda E, mc=mc, pQ=pQ: E.tensor_scalar(
                          mb[mc % 2][:], PS[pQ][:], bqs[:, mc:mc + 1], 0.0,
                          ALU.add, ALU.min),
                      inc="VE")
            ps_cons[pQ] = mn
            me = P.op("scalar",
                      lambda E, mc=mc: E.activation(eb[mc % 2][:],
                                                    mb[mc % 2][:], AF.Exp),
                      waits=[mn], inc="SE")
            mphi = P.op("vector",
                        lambda E, mc=mc: E.tensor_tensor(
                            mb[mc % 2][:], eb[mc % 2][:], rb[mc % 2][:],
                            ALU.add),
                        waits=[me], inc="VE")
            ms = P.op("vector",
                      lambda E, mc=mc: E.tensor_scalar(
                          rb[mc % 2][:], cumb[mc % 2][:], offs[:, mc:mc + 1],
                          sccol[:], ALU.add, ALU.mult),
                      waits=[mcl, m_off, mphi], inc="VE")
            ms_prev[mc] = ms
            ctx_last = P.op("vector",
                            lambda E, mc=mc: E.tensor_tensor(
                                ctx[:, mc * T:(mc + 1) * T], mb[mc % 2][:],
                                rb[mc % 2][:], ALU.mult),
                            inc="VE")
        ms_last_tile = ms_prev
        ln3_marks = None
        for ec in range(CH):
            pO = 4 if ec % 2 == 0 else 5
            mm = proj_mm(woblk, woc, "wo", ec, ec % 4, pO, ctx, [ctx_last],
                         first=(t == 0))
            sl = slice(ec * T, (ec + 1) * T)
            m1 = P.op("vector",
                      lambda E, ec=ec, pO=pO: E.tensor_scalar(
                          rb[ec % 2][:], PS[pO][:], bos[:, ec:ec + 1], None,
                          ALU.add),
                      waits=[mm], inc="VE")
            ps_cons[pO] = m1
            xb_m = P.op("vector",
                        lambda E, ec=ec, sl=sl: E.tensor_tensor(
                            accs[:, sl], rb[ec % 2][:], xsb[:, sl],
                            ALU.add),
                        inc="VE")
            ln3_marks = ln_stats(accs, ec, extra_waits=[xb_m])
        am3 = ln_finish(accs, ln3_marks)
        mstore, last_ev = ffn(accs, xsb, out_v[:, :, tsl], am3, None)
        prev_store = mstore
        prev_ev = last_ev

    st_total = ("ST", P.cnt["ST"])
    P.op("sync", lambda E: None, waits=[st_total])

    from contextlib import ExitStack
    with ExitStack() as stack:
        en = stack.enter_context
        sems = {}
        for s in ("LD", "LG", "ST", "PE", "VE", "SE", "CC1", "CC2"):
            sems[s] = en(nc.semaphore(s))
        block = en(nc.Block())

        @block.sync
        def _(E):
            P.emit(E, "sync", sems)

        @block.gpsimd
        def _(E):
            P.emit(E, "gpsimd", sems)

        @block.vector
        def _(E):
            P.emit(E, "vector", sems)

        @block.scalar
        def _(E):
            P.emit(E, "scalar", sems)

        @block.tensor
        def _(E):
            P.emit(E, "tensor", sems)

    return nc


_NC_CACHE = {}


def kernel(L0_attn_x, L0_attn_Wq, L0_attn_bq, L0_attn_Wk, L0_attn_bk,
           L0_attn_Wv, L0_attn_bv, L0_attn_Wo, L0_attn_bo,
           L0_ffn_W1, L0_ffn_W2, L0_ffn_W3):
    x = np.asarray(L0_attn_x, dtype=np.float32)
    B, L, D_ = x.shape

    def blk(W):
        W = np.asarray(W, dtype=np.float32)
        M, K = W.shape
        return np.ascontiguousarray(
            W.reshape(M // 128, 128, K // 128, 128).transpose(0, 3, 2, 1))

    w1b = blk(L0_ffn_W1)
    w3b = blk(L0_ffn_W3)
    wqb = blk(L0_attn_Wq)
    wkb = blk(L0_attn_Wk)
    wvb = blk(L0_attn_Wv)
    wob = blk(L0_attn_Wo)
    w2t = np.ascontiguousarray(np.asarray(L0_ffn_W2, dtype=np.float32).T)

    def bvec(b):
        return np.ascontiguousarray(
            np.asarray(b, dtype=np.float32).reshape(CH, 128).T)

    bq, bk, bv, bo = map(bvec, (L0_attn_bq, L0_attn_bk, L0_attn_bv, L0_attn_bo))
    h128 = _hadamard128()

    xflat = x.reshape(B * L, D_)
    in_maps = []
    for c in range(NCORES):
        sel = np.zeros((8, 1), np.float32)
        if c % 2 == 1:
            sel[c - 1, 0] = 1.0
        xc = np.ascontiguousarray(xflat[c * TOK:(c + 1) * TOK].T)
        w2slice = np.ascontiguousarray(w2t[:, c * 256:(c + 1) * 256])
        in_maps.append({
            "xT": xc, "w1blk": w1b, "w3blk": w3b, "wqblk": wqb, "wkblk": wkb,
            "wvblk": wvb, "woblk": wob, "w2ts": w2slice,
            "bqh": bq, "bkh": bk, "bvh": bv, "boh": bo,
            "h128h": h128, "selh": sel,
        })

    if "nc" not in _NC_CACHE:
        _NC_CACHE["nc"] = build_program()
    nc = _NC_CACHE["nc"]
    res = run_bass_kernel_spmd(nc, in_maps, core_ids=list(range(NCORES)))
    shards = [np.asarray(res.results[c]["outT"]).T for c in range(NCORES)]
    return np.concatenate(shards, axis=0).reshape(B, L, D_)


# version 18 (final)
# speedup vs baseline: 1.0332x; 1.0332x over previous
import numpy as np
import concourse.bass as bass
import concourse.mybir as mybir
from concourse.bass_utils import run_bass_kernel_spmd

F32 = mybir.dt.float32
BF16 = mybir.dt.bfloat16
AF = mybir.ActivationFunctionType
ALU = mybir.AluOpType
f32 = np.float32

NCORES = 8
D = 2048
HID = 8192
CH = D // 128
HC = HID // 128
TOK = 2048
T = 512
NT = TOK // T
SCALE = 128 ** -0.5
INV_D = 1.0 / D
EPS = 1e-5
FWHT_SCALE = HID ** -0.5

ENGS = ("sync", "gpsimd", "vector", "scalar", "tensor")
DMA_SEMS = {"LD", "LG", "ST"}


class Plan:
    def __init__(self):
        self.ops = {e: [] for e in ENGS}
        self.cnt = {}
        self.last_wait = {e: {} for e in ENGS}

    def op(self, eng, fn, waits=(), inc=None):
        mark = None
        amt = 0
        if inc is not None:
            amt = 16 if inc in DMA_SEMS else 1
            self.cnt[inc] = self.cnt.get(inc, 0) + amt
            mark = (inc, self.cnt[inc])
        real_waits = []
        lw = self.last_wait[eng]
        for w in waits:
            if w is None:
                continue
            s, v = w
            if lw.get(s, -1) >= v:
                continue
            lw[s] = v
            real_waits.append((s, v))
        self.ops[eng].append((real_waits, fn, inc, amt))
        return mark

    def emit(self, eng_obj, eng_name, sems):
        for waits, fn, inc, amt in self.ops[eng_name]:
            for (s, v) in waits:
                eng_obj.wait_ge(sems[s], v)
            ins = fn(eng_obj)
            if inc is not None:
                ins.then_inc(sems[inc], amt)


def _hadamard128():
    pc = np.array([bin(v).count("1") for v in range(128)])
    i = np.arange(128)
    return ((-1.0) ** pc[(i[:, None] & i[None, :])]).astype(np.float32)


def build_program(debug=False):
    nc = bass.Bass("TRN2", target_bir_lowering=False)

    xT = nc.dram_tensor("xT", [D, TOK], F32, kind="ExternalInput")
    w1blk = nc.dram_tensor("w1blk", [HC, 128, CH, 128], F32, kind="ExternalInput")
    w3blk = nc.dram_tensor("w3blk", [HC, 128, CH, 128], F32, kind="ExternalInput")
    wqblk = nc.dram_tensor("wqblk", [CH, 128, CH, 128], F32, kind="ExternalInput")
    wkblk = nc.dram_tensor("wkblk", [CH, 128, CH, 128], F32, kind="ExternalInput")
    wvblk = nc.dram_tensor("wvblk", [CH, 128, CH, 128], F32, kind="ExternalInput")
    woblk = nc.dram_tensor("woblk", [CH, 128, CH, 128], F32, kind="ExternalInput")
    w2ts = nc.dram_tensor("w2ts", [HID, 256], F32, kind="ExternalInput")
    bqh = nc.dram_tensor("bqh", [128, CH], F32, kind="ExternalInput")
    bkh = nc.dram_tensor("bkh", [128, CH], F32, kind="ExternalInput")
    bvh = nc.dram_tensor("bvh", [128, CH], F32, kind="ExternalInput")
    boh = nc.dram_tensor("boh", [128, CH], F32, kind="ExternalInput")
    h128h = nc.dram_tensor("h128h", [128, 128], F32, kind="ExternalInput")
    selh = nc.dram_tensor("selh", [8, 1], F32, kind="ExternalInput")
    outT = nc.dram_tensor("outT", [D, TOK], F32, kind="ExternalOutput")

    dbg_kind = "ExternalOutput" if debug else "Internal"
    xA = nc.dram_tensor("xA", [D, TOK], F32, kind=dbg_kind)
    cumD = nc.dram_tensor("cumD", [D, TOK], F32, kind=dbg_kind)
    w1c = nc.dram_tensor("w1c", [HC, 128, CH, 128], BF16, kind="Internal")
    w3c = nc.dram_tensor("w3c", [HC, 128, CH, 128], BF16, kind="Internal")
    wqc = nc.dram_tensor("wqc", [CH, 128, CH, 128], BF16, kind="Internal")
    wkc = nc.dram_tensor("wkc", [CH, 128, CH, 128], BF16, kind="Internal")
    wvc = nc.dram_tensor("wvc", [CH, 128, CH, 128], BF16, kind="Internal")
    woc = nc.dram_tensor("woc", [CH, 128, CH, 128], BF16, kind="Internal")
    w2loc = nc.dram_tensor("w2loc", [256, HID], BF16, kind="Internal")
    w2p = nc.dram_tensor("w2p", [2048, HID], BF16, kind="Internal",
                         addr_space="Shared")
    w2pd = (nc.dram_tensor("w2pd", [2048, HID], BF16, kind="ExternalOutput")
            if debug else None)
    cc_in = nc.dram_tensor("cc_in", [1, D], F32, kind="Internal")
    cc_out = nc.dram_tensor("cc_out", [8, D], F32, kind="Internal",
                            addr_space="Shared")
    row_scr = nc.dram_tensor("row_scr", [1, D], F32, kind="Internal")

    xT_v = xT.rearrange("(c p) t -> p c t", p=128)
    xA_v = xA.rearrange("(c p) t -> p c t", p=128)
    cum_v = cumD.rearrange("(c p) t -> p c t", p=128)
    out_v = outT.rearrange("(c p) t -> p c t", p=128)
    w2ts_v = w2ts.rearrange("(jc p) e -> p jc e", p=128)

    _probe = nc.alloc_sbuf_tensor("base_probe", [128, 32], mybir.dt.uint8)
    _base = nc.lookup_mloc(_probe).addr + 32
    alloc_top = [_base]

    def sa(name, shape, dtype, at=None):
        elt = 2 if dtype == BF16 else 4
        nbytes = int(np.prod(shape[1:])) * elt
        if at is None:
            off = (alloc_top[0] + 31) & ~31
            alloc_top[0] = off + nbytes
        else:
            off = at
        return nc.alloc_sbuf_tensor_at(name, shape, dtype, offset=off)

    xsb = sa("xsb", [128, CH * T], F32)
    xnb = sa("xnb", [128, CH * T], BF16)
    gsb = sa("gsb", [128, HC * T], BF16)
    accs = sa("accs", [128, CH * T], F32)
    wb = [sa(f"wb{i}", [128, CH * 128], BF16) for i in range(4)]
    w2sb = [sa(f"w2sb{i}", [128, 4096], BF16) for i in range(2)]
    base_after = alloc_top[0]

    def off_of(t_):
        return t_.manual_sbuf_range[0]

    pstrip = [
        sa("pstripA", [128, HC * 128], F32, at=off_of(gsb)),
        sa("pstripB", [128, HC * 128], F32, at=off_of(accs)),
    ]
    bstg = sa("bstg", [128, HC * 128], BF16, at=off_of(xsb))
    stg2 = sa("stg2", [128, HC * 128], BF16, at=off_of(xsb) + 16384)
    ctx = sa("ctx", [128, CH * T], BF16, at=off_of(gsb) + 32768)
    gset = sa("gset", [8, D], BF16, at=off_of(accs))
    rowsum = sa("rowsum", [1, D], F32, at=off_of(accs) + 16384)

    alloc_top[0] = base_after
    scan1 = sa("scan1", [128, T], F32)
    onecol = sa("onecol", [128, 1], BF16)
    onerow = sa("onerow", [1, 128], BF16)
    h128b = sa("h128b", [128, 128], BF16)
    halfc = sa("halfc", [128, 1], F32)
    sccol = sa("sccol", [128, 1], F32)
    bqs = sa("bqs", [128, CH], F32)
    bks = sa("bks", [128, CH], F32)
    bvs = sa("bvs", [128, CH], F32)
    bos = sa("bos", [128, CH], F32)
    sel_sb = sa("sel_sb", [8, 1], F32)
    sel_b = sa("sel_b", [8, 1], BF16)
    offs = sa("offs", [128, CH], F32)
    car = [sa(f"car{i}", [128, CH], F32) for i in range(2)]
    csum = sa("csum", [128, 1], F32)
    sqb = [sa(f"sqb{i}", [128, T], BF16) for i in range(2)]
    s1b = [sa(f"s1b{i}", [128, T], F32) for i in range(2)]
    xstage = [sa(f"xstage{i}", [128, T], BF16, at=off_of(s1b[i]))
              for i in range(2)]
    rb_ = sa("rbS", [128, T], F32)
    mb_ = sa("mbS", [128, T], F32)
    eb_ = sa("ebS", [128, T], F32)
    rb = [rb_, rb_]
    mb = [mb_, mb_]
    eb = [eb_, eb_]
    cumb = [sa(f"cumb{i}", [128, T], F32) for i in range(2)]
    rows = sa("rows", [1, 4 * T], F32)
    rows_b = sa("rows_b", [1, 2 * T], BF16)
    assert alloc_top[0] <= _base + 212800 - 32, \
        f"sbuf over budget: {alloc_top[0] - _base}"

    PS = [nc.alloc_psum_tensor(f"P{i}", [128, T], F32) for i in range(8)]

    P = Plan()
    ps_cons = {i: None for i in range(8)}
    wb_cons = {i: None for i in range(4)}
    wb_stm = {i: None for i in range(4)}
    w2_cons = {i: None for i in range(2)}
    wcache = {k: {} for k in ("w1", "w3", "wq", "wk", "wv", "wo")}

    P.op("vector", lambda E: E.memset(scan1[:], 1.0))
    P.op("vector", lambda E: E.memset(onecol[:], 1.0))
    P.op("vector", lambda E: E.memset(onerow[:], 1.0))
    P.op("vector", lambda E: E.memset(halfc[:], 0.5))
    P.op("vector", lambda E: E.memset(sccol[:], SCALE))
    P.op("vector", lambda E: E.memset(car[0][:], 0.0), inc="VE")
    m_small = []
    for dst, src in ((bqs, bqh), (bks, bkh), (bvs, bvh), (bos, boh),
                     (sel_sb, selh)):
        m_small.append(P.op("sync",
                            lambda E, d=dst, s=src: E.dma_start(d[:], s.ap()),
                            inc="LD"))
    m_h128 = P.op("gpsimd", lambda E: E.dma_start(h128b[:], h128h.ap()),
                  inc="LG")
    m_selb = P.op("vector", lambda E: E.tensor_copy(sel_b[:], sel_sb[:]),
                  waits=[m_small[4]], inc="VE")

    prep_store_marks = []
    prev_evac = None
    prev_pe = None
    for ls in range(2):
        src = w2ts_v[:, :, ls * 128:(ls + 1) * 128]
        lw = [prev_pe] if ls else []
        mload = P.op("sync", lambda E, s=src: E.dma_start(pstrip[0][:], s),
                     waits=lw, inc="LD")
        cur = 0
        mst = mload
        last_bf = None
        for st_i, dch in enumerate([1, 2, 4, 8, 16, 32]):
            sv = pstrip[cur].rearrange("p (q two t) -> p q two t", two=2,
                                       t=dch * 128)
            dv = pstrip[1 - cur].rearrange("p (q two t) -> p q two t", two=2,
                                           t=dch * 128)
            w = [mst] if st_i == 0 else []
            P.op("vector",
                 lambda E, d=dv, s=sv: E.tensor_tensor(
                     d[:, :, 0, :], s[:, :, 0, :], s[:, :, 1, :], ALU.add),
                 waits=w, inc="VE")
            last_bf = P.op("vector",
                           lambda E, d=dv, s=sv: E.tensor_tensor(
                               d[:, :, 1, :], s[:, :, 0, :], s[:, :, 1, :],
                               ALU.subtract),
                           inc="VE")
            cur = 1 - cur
        mconv = P.op("vector",
                     lambda E, s=pstrip[cur]: E.tensor_copy(bstg[:], s[:]),
                     waits=[last_bf, prev_pe], inc="VE")
        for q in range(16):
            w = [mconv, m_h128, prev_evac]
            mpe = P.op("tensor",
                       lambda E, q=q: E.matmul(PS[6][:], h128b[:],
                                               bstg[:, q * 512:(q + 1) * 512],
                                               start=True, stop=True),
                       waits=w, inc="PE")
            prev_pe = mpe
            ew = [mpe] + ([prep_store_marks[-1]] if (q == 0 and prep_store_marks)
                          else [])
            prev_evac = P.op("scalar",
                             lambda E, q=q: E.activation(
                                 stg2[:, q * 512:(q + 1) * 512], PS[6][:],
                                 AF.Identity, scale=f32(FWHT_SCALE)),
                             waits=ew, inc="SE")
        mw = P.op("sync",
                  lambda E, ls=ls: E.dma_start(
                      w2loc[ls * 128:(ls + 1) * 128, :], stg2[:]),
                  waits=[prev_evac], inc="ST")
        prep_store_marks.append(mw)
        ps_cons[6] = prev_evac
    m_ag1 = P.op("gpsimd",
                 lambda E: E.collective_compute(
                     "AllGather", ALU.bypass, replica_groups=[list(range(8))],
                     ins=[w2loc.ap().opt()], outs=[w2p.ap().opt()]),
                 waits=prep_store_marks, inc="CC1")

    if w2pd is not None:
        P.op("sync", lambda E: E.dma_start(w2pd.ap(), w2p.ap()),
             waits=[("CC1", 1)], inc="ST")

    sq_pe = []

    def ln_stats(src_sb, c, extra_waits=()):
        sl = slice(c * T, (c + 1) * T)
        mcopy = P.op("scalar",
                     lambda E, sl=sl, b=xstage[c % 2]: E.copy(b[:], src_sb[:, sl]),
                     waits=list(extra_waits) + ([sq_pe[-4]] if len(sq_pe) >= 4
                                                else []),
                     inc="SE")
        msq = P.op("scalar",
                   lambda E, sl=sl, b=sqb[c % 2]: E.activation(
                       b[:], src_sb[:, sl], AF.Square),
                   waits=[sq_pe[-3]] if len(sq_pe) >= 3 else [], inc="SE")
        m1 = P.op("tensor",
                  lambda E, c=c, b=xstage[c % 2]: E.matmul(
                      PS[6][0:1, :], onecol[:], b[:],
                      start=(c == 0), stop=(c == CH - 1)),
                  waits=[mcopy] + ([ps_cons[6]] if c == 0 else []), inc="PE")
        m2 = P.op("tensor",
                  lambda E, c=c, b=sqb[c % 2]: E.matmul(
                      PS[7][0:1, :], onecol[:], b[:],
                      start=(c == 0), stop=(c == CH - 1)),
                  waits=[msq] + ([ps_cons[7]] if c == 0 else []), inc="PE")
        sq_pe.append(m1)
        sq_pe.append(m2)
        return m1, m2

    def ln_finish(src_sb, sum_marks):
        sum1, sum2 = sum_marks
        r_m = rows[0:1, 0:T]
        r_v = rows[0:1, T:2 * T]
        r_sd = rows[0:1, 2 * T:3 * T]
        r_a = rows[0:1, 3 * T:4 * T]
        r_bn = rows[0:1, T:2 * T]
        r_t = rows[0:1, 2 * T:3 * T]
        mm1 = P.op("scalar",
                   lambda E: E.activation(r_m, PS[6][0:1, :], AF.Identity,
                                          scale=f32(INV_D)),
                   waits=[sum1], inc="SE")
        mm2 = P.op("scalar",
                   lambda E: E.activation(r_v, PS[7][0:1, :], AF.Identity,
                                          scale=f32(INV_D)),
                   waits=[sum2], inc="SE")
        ps_cons[6] = mm1
        ps_cons[7] = mm2
        mt = P.op("vector", lambda E: E.tensor_tensor(r_t, r_m, r_m, ALU.mult),
                  waits=[mm1], inc="VE")
        mv = P.op("vector",
                  lambda E: E.tensor_tensor(r_v, r_v, r_t, ALU.subtract),
                  waits=[mm2, mt], inc="VE")
        msd = P.op("scalar",
                   lambda E: E.activation(r_sd, r_v, AF.Ln, bias=f32(EPS)),
                   waits=[mv], inc="SE")
        mra = P.op("scalar",
                   lambda E: E.activation(r_a, r_sd, AF.Exp, scale=f32(-0.5)),
                   waits=[msd], inc="SE")
        mab = P.op("scalar",
                   lambda E: E.copy(rows_b[0:1, 0:T], r_a),
                   waits=[mra], inc="SE")
        mbn0 = P.op("vector",
                    lambda E: E.tensor_tensor(r_t, r_m, r_a, ALU.mult),
                    waits=[mra], inc="VE")
        mbn = P.op("vector",
                   lambda E: E.tensor_scalar(r_bn, r_t, -1.0, None, ALU.mult),
                   waits=[mbn0], inc="VE")
        mbb = P.op("vector",
                   lambda E: E.tensor_copy(rows_b[0:1, T:2 * T], r_bn),
                   waits=[mbn], inc="VE")
        mba = P.op("tensor",
                   lambda E: E.matmul(PS[4][:], onerow[:], rows_b[0:1, 0:T],
                                      start=True, stop=True),
                   waits=[mab, ps_cons[4]], inc="PE")
        mbbc = P.op("tensor",
                    lambda E: E.matmul(PS[5][:], onerow[:],
                                       rows_b[0:1, T:2 * T],
                                       start=True, stop=True),
                    waits=[mbb, ps_cons[5]], inc="PE")
        am = None
        am_l = []
        for c in range(CH):
            sl = slice(c * T, (c + 1) * T)
            P.op("vector",
                 lambda E, sl=sl, b=rb[c % 2]: E.tensor_tensor(
                     b[:], src_sb[:, sl], PS[4][:], ALU.mult),
                 waits=[mba], inc="VE")
            am = P.op("vector",
                      lambda E, sl=sl, b=rb[c % 2]: E.tensor_tensor(
                          xnb[:, sl], b[:], PS[5][:], ALU.add),
                      waits=[mbbc], inc="VE")
            am_l.append(am)
        ps_cons[4] = am
        ps_cons[5] = am
        return am, am_l

    def ln(src_sb, pre_waits, chunk_waits=None):
        sm = None
        for c in range(CH):
            w = list(pre_waits) if c == 0 else []
            if chunk_waits is not None:
                w.append(chunk_waits[c])
            sm = ln_stats(src_sb, c, extra_waits=w)
        return ln_finish(src_sb, sm)

    g_marks = {}

    def wload(eng, slot, kind, f32blk, cblk, mc, first):
        if first:
            ml = P.op(eng,
                      lambda E, s=slot, mc=mc: E.dma_start(wb[s][:],
                                                           f32blk[mc]),
                      waits=[wb_cons[slot], wb_stm[slot]], inc="LG")
            mst = P.op("scalar",
                       lambda E, s=slot, mc=mc: E.dma_start(cblk[mc],
                                                            wb[s][:]),
                       waits=[ml], inc="ST")
            wcache[kind][mc] = mst
            wb_stm[slot] = mst
        else:
            ml = P.op(eng,
                      lambda E, s=slot, mc=mc: E.dma_start(wb[s][:], cblk[mc]),
                      waits=[wb_cons[slot], wb_stm[slot], wcache[kind][mc]],
                      inc="LG")
        return ml

    def ffn(src_res, dst_sb, store_view, apply_mark, dst_free_wait,
            post_ec=None, first=False, chunk_store=False):
        for mc in range(HC):
            s0, s1_ = (2 * mc) % 4, (2 * mc + 1) % 4
            l1 = wload("gpsimd", s0, "w1", w1blk, w1c, mc, first)
            l3 = wload("gpsimd", s1_, "w3", w3blk, w3c, mc, first)
            pA = 0 if mc % 2 == 0 else 2
            pB = 1 if mc % 2 == 0 else 3
            mmA = mmB = None

            def emitA(kc, s=s0, pA=pA, l1=l1, mc=mc):
                ksl = slice(kc * T, (kc + 1) * T)
                wA = [l1, ps_cons[pA]] if kc == 0 else []
                if mc == 0:
                    wA = wA + [apply_mark[kc]]
                return P.op("tensor",
                            lambda E, kc=kc, s=s, ksl=ksl, pA=pA: E.matmul(
                                PS[pA][:], wb[s][:, kc * 128:(kc + 1) * 128],
                                xnb[:, ksl], start=(kc == 0),
                                stop=(kc == CH - 1)),
                            waits=wA, inc="PE")

            def emitB(kc, s=s1_, pB=pB, l3=l3):
                ksl = slice(kc * T, (kc + 1) * T)
                wB = [l3, ps_cons[pB]] if kc == 0 else []
                return P.op("tensor",
                            lambda E, kc=kc, s=s, ksl=ksl, pB=pB: E.matmul(
                                PS[pB][:], wb[s][:, kc * 128:(kc + 1) * 128],
                                xnb[:, ksl], start=(kc == 0),
                                stop=(kc == CH - 1)),
                            waits=wB, inc="PE")

            if mc == 0:
                for kc in range(CH):
                    mmA = emitA(kc)
                    mmB = emitB(kc)
            else:
                for kc in range(CH):
                    mmA = emitA(kc)
                for kc in range(CH):
                    mmB = emitB(kc)
            wb_cons[s0], wb_cons[s1_] = mmA, mmB
            wsl = [mmA] + ([g_marks[mc - 2]] if mc - 2 in g_marks else [])
            msilu = P.op("scalar",
                         lambda E, mc=mc, pA=pA: E.activation(
                             s1b[mc % 2][:], PS[pA][:], AF.Silu),
                         waits=wsl, inc="SE")
            ps_cons[pA] = msilu
            mg = P.op("vector",
                      lambda E, mc=mc, pB=pB: E.tensor_tensor(
                          gsb[:, mc * T:(mc + 1) * T], s1b[mc % 2][:],
                          PS[pB][:], ALU.mult),
                      waits=[msilu, mmB], inc="VE")
            ps_cons[pB] = mg
            g_marks[mc] = mg
        last_g = g_marks[HC - 1]
        ev_marks = []
        st_marks = []
        for ec in range(CH):
            pD = 4 if ec % 2 == 0 else 5
            hl = []
            for h in range(2):
                lw = [w2_cons[h]]
                if ec == 0 and h == 0:
                    lw.append(("CC1", 1))
                ml = P.op("sync",
                          lambda E, ec=ec, h=h: E.dma_start(
                              w2sb[h][:],
                              w2p[ec * 128:(ec + 1) * 128,
                                  h * 4096:(h + 1) * 4096]),
                          waits=lw, inc="LD")
                hl.append(ml)
            mmD = None
            for jc in range(HC):
                h = jc // 32
                jl = jc % 32
                w = []
                if jc == 0:
                    w = [hl[0], last_g, ps_cons[pD]]
                elif jc == 32:
                    w = [hl[1]]
                mmD = P.op("tensor",
                           lambda E, jc=jc, h=h, jl=jl, pD=pD: E.matmul(
                               PS[pD][:], w2sb[h][:, jl * 128:(jl + 1) * 128],
                               gsb[:, jc * T:(jc + 1) * T],
                               start=(jc == 0), stop=(jc == HC - 1)),
                           waits=w, inc="PE")
                if jc == 31:
                    w2_cons[0] = mmD
            w2_cons[1] = mmD
            sl = slice(ec * T, (ec + 1) * T)
            wev = [mmD] + ([dst_free_wait] if ec == 0 else [])
            mev = P.op("vector",
                       lambda E, sl=sl, pD=pD: E.scalar_tensor_tensor(
                           dst_sb[:, sl], PS[pD][:], halfc[:],
                           src_res[:, sl], ALU.mult, ALU.add),
                       waits=wev, inc="VE")
            ps_cons[pD] = mev
            ev_marks.append(mev)
            if chunk_store:
                st_marks.append(P.op("scalar",
                                     lambda E, ec=ec, sl=sl: E.dma_start(
                                         store_view[:, ec:ec + 1, :],
                                         dst_sb[:, sl]),
                                     waits=[mev], inc="ST"))
            if post_ec is not None:
                post_ec(ec, mev)
        if chunk_store:
            return st_marks, ev_marks[-1]
        mstore = P.op("sync", lambda E: E.dma_start(store_view, dst_sb[:]),
                      waits=[ev_marks[-1]], inc="ST")
        return mstore, ev_marks[-1]

    def proj_mm(blk, cblk, kind, mc, slot, pbank, rhs_sb, first_waits,
                first=False, apply_l=None):
        ml = wload("gpsimd", slot, kind, blk, cblk, mc, first)
        mm = None
        for kc in range(CH):
            ksl = slice(kc * T, (kc + 1) * T)
            w = [ml, ps_cons[pbank]] + list(first_waits) if kc == 0 else []
            if apply_l is not None:
                w = list(w) + [apply_l[kc]]
            mm = P.op("tensor",
                      lambda E, kc=kc, s=slot, ksl=ksl, pb=pbank: E.matmul(
                          PS[pb][:], wb[s][:, kc * 128:(kc + 1) * 128],
                          rhs_sb[:, ksl], start=(kc == 0),
                          stop=(kc == CH - 1)),
                      waits=w, inc="PE")
        wb_cons[slot] = mm
        return mm

    def proj_pair(blkA, cblkA, kindA, blkB, cblkB, kindB, mc, sA, sB, pA, pB,
                  rhs_sb, apply_l, first):
        lA = wload("gpsimd", sA, kindA, blkA, cblkA, mc, first)
        lB = wload("gpsimd", sB, kindB, blkB, cblkB, mc, first)
        mA = mB = None

        def emA(kc):
            ksl = slice(kc * T, (kc + 1) * T)
            wA = [lA, ps_cons[pA]] if kc == 0 else []
            if apply_l is not None and mc == 0:
                wA = wA + [apply_l[kc]]
            return P.op("tensor",
                        lambda E, kc=kc, s=sA, ksl=ksl, p=pA: E.matmul(
                            PS[p][:], wb[s][:, kc * 128:(kc + 1) * 128],
                            rhs_sb[:, ksl], start=(kc == 0),
                            stop=(kc == CH - 1)),
                        waits=wA, inc="PE")

        def emB(kc):
            ksl = slice(kc * T, (kc + 1) * T)
            wB = [lB, ps_cons[pB]] if kc == 0 else []
            return P.op("tensor",
                        lambda E, kc=kc, s=sB, ksl=ksl, p=pB: E.matmul(
                            PS[p][:], wb[s][:, kc * 128:(kc + 1) * 128],
                            rhs_sb[:, ksl], start=(kc == 0),
                            stop=(kc == CH - 1)),
                        waits=wB, inc="PE")

        if mc == 0:
            for kc in range(CH):
                mA = emA(kc)
                mB = emB(kc)
        else:
            for kc in range(CH):
                mA = emA(kc)
            for kc in range(CH):
                mB = emB(kc)
        wb_cons[sA], wb_cons[sB] = mA, mB
        return mA, mB

    xa_store = []
    cum_store = []
    prev_ffn_ev = None
    prev_xa_store = None
    am_next = None
    for t in range(NT):
        tsl = slice(t * T, (t + 1) * T)
        if t == 0:
            mload = P.op("sync",
                         lambda E: E.dma_start(xsb[:], xT_v[:, :, 0:T]),
                         waits=list(prep_store_marks), inc="LD")
            am, am_l = ln(xsb, [mload])
        else:
            am, am_l = am_next
        ln2_marks = [None]

        def post_ec(ec, mev, _m=ln2_marks):
            _m[0] = ln_stats(accs, ec, extra_waits=[mev])

        mstore, last_ev = ffn(xsb, accs, xA_v[:, :, tsl], am_l, prev_xa_store,
                              post_ec=post_ec, first=(t == 0))
        xa_store.append(mstore)
        prev_xa_store = mstore
        am2, am2_l = ln_finish(accs, ln2_marks[0])
        mload_next = None
        if t + 1 < NT:
            mload_next = P.op("sync",
                              lambda E, t=t: E.dma_start(
                                  xsb[:], xT_v[:, :, (t + 1) * T:(t + 2) * T]),
                              waits=[last_ev], inc="LD")
        ln1_marks = None
        for mc in range(CH):
            pK = 0 if mc % 2 == 0 else 2
            pV = 1 if mc % 2 == 0 else 3
            mk, mv_ = proj_pair(wkblk, wkc, "wk", wvblk, wvc, "wv", mc,
                                (2 * mc) % 4, (2 * mc + 1) % 4, pK, pV,
                                xnb, am2_l, t == 0)
            mr = P.op("vector",
                      lambda E, mc=mc, pK=pK: E.tensor_scalar(
                          rb[mc % 2][:], PS[pK][:], bks[:, mc:mc + 1], 0.0,
                          ALU.add, ALU.max),
                      waits=[mk], inc="VE")
            mn = P.op("vector",
                      lambda E, mc=mc, pK=pK: E.tensor_scalar(
                          mb[mc % 2][:], PS[pK][:], bks[:, mc:mc + 1], 0.0,
                          ALU.add, ALU.min),
                      inc="VE")
            ps_cons[pK] = mn
            me = P.op("scalar",
                      lambda E, mc=mc: E.activation(eb[mc % 2][:],
                                                    mb[mc % 2][:], AF.Exp),
                      waits=[mn], inc="SE")
            mphi = P.op("vector",
                        lambda E, mc=mc: E.tensor_tensor(
                            mb[mc % 2][:], eb[mc % 2][:], rb[mc % 2][:],
                            ALU.add),
                        waits=[me], inc="VE")
            mvv = P.op("vector",
                       lambda E, mc=mc, pV=pV: E.tensor_scalar(
                           rb[mc % 2][:], PS[pV][:], bvs[:, mc:mc + 1], None,
                           ALU.add),
                       waits=[mv_, mphi], inc="VE")
            ps_cons[pV] = mvv
            P.op("vector",
                 lambda E, mc=mc: E.tensor_tensor(
                     eb[mc % 2][:], mb[mc % 2][:], rb[mc % 2][:], ALU.mult),
                 inc="VE")
            P.op("vector",
                 lambda E, mc=mc: E.reduce_sum(csum[:], eb[mc % 2][:],
                                               mybir.AxisListType.X),
                 inc="VE")
            wsc = [cum_store[-2]] if len(cum_store) >= 2 else []
            cin_, cout_ = car[t % 2], car[(t + 1) % 2]
            mcum = P.op("vector",
                        lambda E, mc=mc, cin_=cin_: E.tensor_tensor_scan(
                            cumb[mc % 2][:], scan1[:], eb[mc % 2][:],
                            cin_[:, mc:mc + 1], ALU.mult, ALU.add),
                        waits=wsc, inc="VE")
            P.op("vector",
                 lambda E, mc=mc, cin_=cin_, cout_=cout_: E.tensor_tensor(
                     cout_[:, mc:mc + 1], csum[:], cin_[:, mc:mc + 1],
                     ALU.add),
                 inc="VE")
            mcs = P.op("sync",
                       lambda E, mc=mc, t=t: E.dma_start(
                           cum_v[:, mc:mc + 1, t * T:(t + 1) * T],
                           cumb[mc % 2][:]),
                       waits=[mcum], inc="ST")
            cum_store.append(mcs)
            if t + 1 < NT:
                ln1_marks = ln_stats(
                    xsb, mc,
                    extra_waits=[mload_next] if mc == 0 else ())
        if t + 1 < NT:
            am_next = ln_finish(xsb, ln1_marks)
        prev_ffn_ev = last_ev

    mloadC0 = P.op("sync",
                   lambda E: E.dma_start(xsb[:], xA_v[:, :, 0:T]),
                   waits=[prev_ffn_ev, xa_store[0]], inc="LD")
    ve_all = ("VE", P.cnt["VE"])
    pe_all = ("PE", P.cnt["PE"])
    cc_small = []
    for c in range(CH):
        mcc = P.op("sync",
                   lambda E, c=c: E.dma_start(
                       cc_in[0:1, c * 128:(c + 1) * 128],
                       car[NT % 2][:, c:c + 1]),
                   waits=[ve_all] if c == 0 else [], inc="ST")
        cc_small.append(mcc)
    P.op("gpsimd",
         lambda E: E.collective_compute(
             "AllGather", ALU.bypass, replica_groups=[list(range(8))],
             ins=[cc_in.ap().opt()], outs=[cc_out.ap().opt()]),
         waits=cc_small, inc="CC2")
    amC0, amC0_l = ln(xsb, [mloadC0])
    for mc in range(CH):
        pQ = 0 if mc % 2 == 0 else 2
        mq = proj_mm(wqblk, wqc, "wq", mc, mc % 4, pQ, xnb, [],
                     first=True, apply_l=(amC0_l if mc == 0 else None))
        mr = P.op("vector",
                  lambda E, mc=mc, pQ=pQ: E.tensor_scalar(
                      rb[mc % 2][:], PS[pQ][:], bqs[:, mc:mc + 1], 0.0,
                      ALU.add, ALU.max),
                  waits=[mq], inc="VE")
        mn = P.op("vector",
                  lambda E, mc=mc, pQ=pQ: E.tensor_scalar(
                      mb[mc % 2][:], PS[pQ][:], bqs[:, mc:mc + 1], 0.0,
                      ALU.add, ALU.min),
                  inc="VE")
        ps_cons[pQ] = mn
        me = P.op("scalar",
                  lambda E, mc=mc: E.activation(eb[mc % 2][:],
                                                mb[mc % 2][:], AF.Exp),
                  waits=[mn], inc="SE")
        P.op("vector",
             lambda E, mc=mc: E.tensor_tensor(
                 ctx[:, mc * T:(mc + 1) * T], eb[mc % 2][:], rb[mc % 2][:],
                 ALU.add),
             waits=[me], inc="VE")
    m_gl = P.op("gpsimd", lambda E: E.dma_start(gset[:], cc_out.ap()),
                waits=[("CC2", 1), xa_store[-1], ve_all, pe_all], inc="LG")
    selbank = [6, 7, 1, 3]
    cps = []
    for j in range(4):
        b = selbank[j]
        mmj = P.op("tensor",
                   lambda E, j=j, b=b: E.matmul(
                       PS[b][0:1, :], sel_b[:],
                       gset[:, j * T:(j + 1) * T], start=True, stop=True),
                   waits=[m_gl, m_selb, ps_cons[b]], inc="PE")
        cpj = P.op("scalar",
                   lambda E, j=j, b=b: E.activation(
                       rowsum[0:1, j * T:(j + 1) * T], PS[b][0:1, :],
                       AF.Identity),
                   waits=[mmj], inc="SE")
        ps_cons[b] = cpj
        cps.append(cpj)
    m_rs = P.op("scalar", lambda E: E.dma_start(row_scr.ap(), rowsum[0:1, :]),
                waits=cps, inc="ST")
    m_off = None
    for c in range(CH):
        m_off = P.op("scalar",
                     lambda E, c=c: E.dma_start(
                         offs[:, c:c + 1],
                         row_scr[0:1, c * 128:(c + 1) * 128]),
                     waits=[m_rs] if c == 0 else [], inc="LD")

    ms_prev = {}
    ctx_last = None
    for mc in range(CH):
        wcl = ([cum_store[-1]] if mc == 0 else [])
        if mc - 2 in ms_prev:
            wcl = wcl + [ms_prev[mc - 2]]
        mcl = P.op("sync",
                   lambda E, mc=mc: E.dma_start(
                       cumb[mc % 2][:], cum_v[:, mc:mc + 1, 0:T]),
                   waits=wcl, inc="LD")
        ms = P.op("vector",
                  lambda E, mc=mc: E.tensor_scalar(
                      rb[mc % 2][:], cumb[mc % 2][:], offs[:, mc:mc + 1],
                      sccol[:], ALU.add, ALU.mult),
                  waits=[mcl, m_off], inc="VE")
        ms_prev[mc] = ms
        ctx_last = P.op("vector",
                        lambda E, mc=mc: E.tensor_tensor(
                            ctx[:, mc * T:(mc + 1) * T],
                            ctx[:, mc * T:(mc + 1) * T], rb[mc % 2][:],
                            ALU.mult),
                        inc="VE")
    ms_last_tile = ms_prev

    st_prev = None
    for t in range(NT):
        tsl = slice(t * T, (t + 1) * T)
        if t > 0:
            loads = []
            for c in range(CH):
                w = [st_prev[c]] + ([xa_store[t]] if c == 0 else [])
                loads.append(P.op("sync",
                                  lambda E, c=c, t=t: E.dma_start(
                                      xsb[:, c * T:(c + 1) * T],
                                      xA_v[:, c:c + 1,
                                           t * T:(t + 1) * T]),
                                  waits=w, inc="LD"))
            am, am_l = ln(xsb, [], chunk_waits=loads)
            ctx_last = None
            ms_prev = {}
            for mc in range(CH):
                pQ = 0 if mc % 2 == 0 else 2
                mq = proj_mm(wqblk, wqc, "wq", mc, mc % 4, pQ, xnb, [],
                             first=False,
                             apply_l=(am_l if mc == 0 else None))
                wcl = [ms_prev[mc - 2]] if mc - 2 in ms_prev else []
                if mc < 2:
                    wcl = [ms_last_tile[mc - 2 + CH]]
                mcl = P.op("sync",
                           lambda E, mc=mc, t=t: E.dma_start(
                               cumb[mc % 2][:],
                               cum_v[:, mc:mc + 1, t * T:(t + 1) * T]),
                           waits=wcl, inc="LD")
                mr = P.op("vector",
                          lambda E, mc=mc, pQ=pQ: E.tensor_scalar(
                              rb[mc % 2][:], PS[pQ][:], bqs[:, mc:mc + 1],
                              0.0, ALU.add, ALU.max),
                          waits=[mq], inc="VE")
                mn = P.op("vector",
                          lambda E, mc=mc, pQ=pQ: E.tensor_scalar(
                              mb[mc % 2][:], PS[pQ][:], bqs[:, mc:mc + 1],
                              0.0, ALU.add, ALU.min),
                          inc="VE")
                ps_cons[pQ] = mn
                me = P.op("scalar",
                          lambda E, mc=mc: E.activation(eb[mc % 2][:],
                                                        mb[mc % 2][:],
                                                        AF.Exp),
                          waits=[mn], inc="SE")
                mphi = P.op("vector",
                            lambda E, mc=mc: E.tensor_tensor(
                                mb[mc % 2][:], eb[mc % 2][:], rb[mc % 2][:],
                                ALU.add),
                            waits=[me], inc="VE")
                ms = P.op("vector",
                          lambda E, mc=mc: E.tensor_scalar(
                              rb[mc % 2][:], cumb[mc % 2][:],
                              offs[:, mc:mc + 1],
                              sccol[:], ALU.add, ALU.mult),
                          waits=[mcl, m_off, mphi], inc="VE")
                ms_prev[mc] = ms
                ctx_last = P.op("vector",
                                lambda E, mc=mc: E.tensor_tensor(
                                    ctx[:, mc * T:(mc + 1) * T],
                                    mb[mc % 2][:], rb[mc % 2][:], ALU.mult),
                                inc="VE")
            ms_last_tile = ms_prev
        ln3_marks = None
        for ec in range(CH):
            pO = 4 if ec % 2 == 0 else 5
            mm = proj_mm(woblk, woc, "wo", ec, ec % 4, pO, ctx, [ctx_last],
                         first=(t == 0))
            sl = slice(ec * T, (ec + 1) * T)
            m1 = P.op("vector",
                      lambda E, ec=ec, pO=pO: E.tensor_scalar(
                          rb[ec % 2][:], PS[pO][:], bos[:, ec:ec + 1], None,
                          ALU.add),
                      waits=[mm], inc="VE")
            ps_cons[pO] = m1
            xb_m = P.op("vector",
                        lambda E, ec=ec, sl=sl: E.tensor_tensor(
                            accs[:, sl], rb[ec % 2][:], xsb[:, sl],
                            ALU.add),
                        inc="VE")
            ln3_marks = ln_stats(accs, ec, extra_waits=[xb_m])
        am3, am3_l = ln_finish(accs, ln3_marks)
        st_prev, last_ev = ffn(accs, xsb, out_v[:, :, tsl], am3_l, None,
                               chunk_store=True)

    st_total = ("ST", P.cnt["ST"])
    P.op("sync", lambda E: None, waits=[st_total])

    from contextlib import ExitStack
    with ExitStack() as stack:
        en = stack.enter_context
        sems = {}
        for s in ("LD", "LG", "ST", "PE", "VE", "SE", "CC1", "CC2"):
            sems[s] = en(nc.semaphore(s))
        block = en(nc.Block())

        @block.sync
        def _(E):
            P.emit(E, "sync", sems)

        @block.gpsimd
        def _(E):
            P.emit(E, "gpsimd", sems)

        @block.vector
        def _(E):
            P.emit(E, "vector", sems)

        @block.scalar
        def _(E):
            P.emit(E, "scalar", sems)

        @block.tensor
        def _(E):
            P.emit(E, "tensor", sems)

    return nc


_NC_CACHE = {}


def kernel(L0_attn_x, L0_attn_Wq, L0_attn_bq, L0_attn_Wk, L0_attn_bk,
           L0_attn_Wv, L0_attn_bv, L0_attn_Wo, L0_attn_bo,
           L0_ffn_W1, L0_ffn_W2, L0_ffn_W3):
    x = np.asarray(L0_attn_x, dtype=np.float32)
    B, L, D_ = x.shape

    def blk(W):
        W = np.asarray(W, dtype=np.float32)
        M, K = W.shape
        return np.ascontiguousarray(
            W.reshape(M // 128, 128, K // 128, 128).transpose(0, 3, 2, 1))

    w1b = blk(L0_ffn_W1)
    w3b = blk(L0_ffn_W3)
    wqb = blk(L0_attn_Wq)
    wkb = blk(L0_attn_Wk)
    wvb = blk(L0_attn_Wv)
    wob = blk(L0_attn_Wo)
    w2t = np.ascontiguousarray(np.asarray(L0_ffn_W2, dtype=np.float32).T)

    def bvec(b):
        return np.ascontiguousarray(
            np.asarray(b, dtype=np.float32).reshape(CH, 128).T)

    bq, bk, bv, bo = map(bvec, (L0_attn_bq, L0_attn_bk, L0_attn_bv, L0_attn_bo))
    h128 = _hadamard128()

    xflat = x.reshape(B * L, D_)
    in_maps = []
    for c in range(NCORES):
        sel = np.zeros((8, 1), np.float32)
        if c % 2 == 1:
            sel[c - 1, 0] = 1.0
        xc = np.ascontiguousarray(xflat[c * TOK:(c + 1) * TOK].T)
        w2slice = np.ascontiguousarray(w2t[:, c * 256:(c + 1) * 256])
        in_maps.append({
            "xT": xc, "w1blk": w1b, "w3blk": w3b, "wqblk": wqb, "wkblk": wkb,
            "wvblk": wvb, "woblk": wob, "w2ts": w2slice,
            "bqh": bq, "bkh": bk, "bvh": bv, "boh": bo,
            "h128h": h128, "selh": sel,
        })

    if "nc" not in _NC_CACHE:
        _NC_CACHE["nc"] = build_program()
    nc = _NC_CACHE["nc"]
    res = run_bass_kernel_spmd(nc, in_maps, core_ids=list(range(NCORES)))
    shards = [np.asarray(res.results[c]["outT"]).T for c in range(NCORES)]
    return np.concatenate(shards, axis=0).reshape(B, L, D_)
